# version 02
# baseline (speedup 1.0000x reference)
import numpy as np
import concourse.bass as bass
import concourse.mybir as mybir
from concourse.alu_op_type import AluOpType
from concourse.bass_utils import run_bass_kernel_spmd

F32 = mybir.dt.float32
BF16 = mybir.dt.bfloat16
AF = mybir.ActivationFunctionType
AX = mybir.AxisListType
A = AluOpType

NB = 2
T = 4096
D = 1024
H = 16
HD = 64
E = 8
TIL = 32
CH = 1024
NCH = T // CH
EPS = 1e-5

MB_C = 0
OST_C = 0
VW_C = 4096
RS_C = 6176
WO_C = 0
WI_C = 8224
GBC_C = 10272
AIN_C0 = 0
AIN_C1 = 4096
YIN_C0 = 8192
YIN_C1 = 10240
TR_C = 12320
GEN_W = TR_C + T

ENGS = ("sync", "gpsimd", "tensor", "scalar", "vector")
EK = {"sync": "X", "gpsimd": "G", "tensor": "PE", "scalar": "ACT", "vector": "DVE"}


def build_program(phases="ABCD", dbg=False):
    nc = bass.Bass("TRN2", target_bir_lowering=False, debug=False)
    x = nc.dram_tensor("x", [NB, T, D], F32, kind="ExternalInput").ap()
    wih = nc.dram_tensor("wih", [E, 32, 128, 1024], F32, kind="ExternalInput").ap()
    woh = nc.dram_tensor("woh", [E, 8, 128, 4096], F32, kind="ExternalInput").ap()
    wgh = nc.dram_tensor("wgh", [128, 64], F32, kind="ExternalInput").ap()
    ident_in = nc.dram_tensor("ident", [128, 128], F32, kind="ExternalInput").ap()
    oh_in = nc.dram_tensor("oh", [8, 1024], F32, kind="ExternalInput").ap()
    cones_in = nc.dram_tensor("cones", [128, 1], F32, kind="ExternalInput").ap()
    crow_in = nc.dram_tensor("crow", [1, 128], F32, kind="ExternalInput").ap()
    cnrow_in = nc.dram_tensor("cnrow", [1, 128], F32, kind="ExternalInput").ap()
    out = nc.dram_tensor("out", [NB, T, D], F32, kind="ExternalOutput").ap()
    okind = "ExternalOutput" if dbg else "Internal"
    attnT = nc.dram_tensor("attnT", [65, H, T], BF16, kind=okind).ap()
    yT = nc.dram_tensor("yT", [D, T], BF16, kind=okind).ap()
    gscr = nc.dram_tensor("gscr", [16, 1], F32, kind="Internal").ap()
    dbg_t = {}
    if dbg:
        for nm, shp, dt in (("dbg_xn", [128, TIL * D], BF16), ("dbg_xnt", [128, 8 * T], BF16),
                            ("dbg_lgt", [128, 256], F32), ("dbg_tr", [128, T], BF16),
                            ("dbg_gb", [128, 16], F32), ("dbg_w16", [128, 512], F32)):
            dbg_t[nm] = nc.dram_tensor(nm, shp, dt, kind="ExternalOutput").ap()

    plan = {e: [] for e in ENGS}
    cnt = {"X": 0, "G": 0, "WI": 0, "SO": 0, "PE": 0, "ACT": 0, "DVE": 0}
    last = {"ACT": 0, "DVE": 0}

    def em(eng, fn, waits=(), inc=False, key=None):
        key = key or EK[eng]
        amt = 16 if key in ("X", "G", "WI", "SO") else 1
        wl = list(waits)
        if key in ("ACT", "DVE"):
            inc = True
            if last[key] > 0:
                wl.append((key, last[key]))
        mil = None
        if inc:
            cnt[key] += amt
            mil = cnt[key]
            if key in ("ACT", "DVE"):
                last[key] = mil
        wd = {}
        for k, v in wl:
            if v > 0:
                wd[k] = max(wd.get(k, 0), v)
        wt = tuple(wd.items())

        def run(e, s, fn=fn, wt=wt, inc=inc, key=key, amt=amt):
            for k, v in wt:
                e.wait_ge(s[k], v)
            instr = fn(e)
            if inc:
                instr.then_inc(s[key], amt)
        plan[eng].append(run)
        return mil

    import contextlib
    st_ = contextlib.ExitStack()
    with st_:
        def _sb(name, shape, dt):
            return st_.enter_context(nc.sbuf_tensor(name, shape, dt))
        XNT = _sb("XNT", [128, 8 * T], BF16)
        BIG = _sb("BIG", [128, TIL * D], BF16)
        GEN = _sb("GEN", [128, GEN_W], BF16)
        U2 = _sb("U2", [128, 2048], F32)
        XT = _sb("XT", [128, 2048], F32)
        YAC = _sb("YAC", [128, 8 * CH], BF16)
        W16 = _sb("W16", [128, 512], F32)
        CMB = _sb("CMB", [128, 48], BF16)
        LGT = _sb("LGT", [128, 256], F32)
        STT = _sb("STT", [128, 16], F32)
        UWT = _sb("UWT", [128, 64], F32)
        MR1 = _sb("MR1", [1, 1024], BF16)
        GAM = _sb("GAM", [128, 24], F32)
        GB = _sb("GB", [128, 16], F32)
        GRW = _sb("GRW", [1, 16], BF16)
        RCP = _sb("RCP", [128, 16], F32)
        IDN = _sb("IDN", [128, 128], BF16)
        OH = _sb("OH", [8, 1024], BF16)
        WG = _sb("WG", [128, 64], BF16)
        CON1 = _sb("CON1", [128, 1], BF16)
        CROW = _sb("CROW", [1, 128], BF16)
        CNR = _sb("CNR", [1, 128], BF16)
        PSF = st_.enter_context(nc.psum_tensor("PSF", [128, 3072], F32))
        PSB = st_.enter_context(nc.psum_tensor("PSB", [128, 2048], BF16))
        SX = st_.enter_context(nc.semaphore("SX"))
        SG = st_.enter_context(nc.semaphore("SG"))
        SPE = st_.enter_context(nc.semaphore("SPE"))
        SACT = st_.enter_context(nc.semaphore("SACT"))
        SDVE = st_.enter_context(nc.semaphore("SDVE"))
        SWI = st_.enter_context(nc.semaphore("SWI"))
        SO = st_.enter_context(nc.semaphore("SO"))
        block = st_.enter_context(nc.Block())
        sems = {"X": SX, "G": SG, "PE": SPE, "ACT": SACT, "DVE": SDVE, "WI": SWI, "SO": SO}

        for dst, src in ((IDN, ident_in), (OH, oh_in), (WG, wgh), (CON1, cones_in),
                         (CROW, crow_in), (CNR, cnrow_in)):
            consts_done = em("gpsimd", lambda e, d=dst, sc=src: e.dma_start(out=d[:], in_=sc), inc=True)

        prev = {"out_x": 0, "d_dve": 0, "c_pe": 0, "c_dve": 0}

        for b in range(NB):
            xmil = [0] * TIL
            xnmil = [0] * TIL
            xntmil = [0] * TIL
            trmil = [0] * TIL
            lgmil = [0] * TIL
            for i in range(TIL):
                sl = i % 2
                waits = [("DVE", xnmil[i - 2] if i >= 2 else 0)]
                if b > 0 and i < 2:
                    waits += [("SO", prev["out_x"]), ("DVE", prev["d_dve"])]
                xmil[i] = em("sync", lambda e, sl=sl, bb=b, ii=i: e.dma_start(
                    out=XT[:, sl * 1024:(sl + 1) * 1024], in_=x[bb, ii * 128:(ii + 1) * 128, :]),
                    waits=waits, inc=True)
                xt = XT[:, sl * 1024:(sl + 1) * 1024]
                em("vector", lambda e, xt=xt: e.reduce_sum(STT[:, 0:1], xt, axis=AX.X),
                   waits=[("X", xmil[i])])
                em("vector", lambda e, xt=xt: e.tensor_tensor(U2[:, 0:1024], xt, xt, A.mult))
                em("vector", lambda e: e.reduce_sum(STT[:, 1:2], U2[:, 0:1024], axis=AX.X))
                em("vector", lambda e: e.tensor_scalar(STT[:, 2:3], STT[:, 0:1], 1.0 / D, None, A.mult, A.bypass))
                em("vector", lambda e: e.tensor_scalar(STT[:, 3:4], STT[:, 1:2], 1.0 / D, EPS, A.mult, A.add))
                em("vector", lambda e: e.tensor_tensor(STT[:, 4:5], STT[:, 2:3], STT[:, 2:3], A.mult))
                var_m = em("vector", lambda e: e.tensor_tensor(STT[:, 5:6], STT[:, 3:4], STT[:, 4:5], A.subtract))
                em("scalar", lambda e: e.activation(STT[:, 6:7], STT[:, 5:6], AF.Ln),
                   waits=[("DVE", var_m)])
                rstd_m = em("scalar", lambda e: e.activation(STT[:, 7:8], STT[:, 6:7], AF.Exp, scale=-0.5))
                xnmil[i] = em("vector", lambda e, xt=xt, ii=i: e.tensor_scalar(
                    BIG[:, ii * D:(ii + 1) * D], xt, STT[:, 2:3], STT[:, 7:8], A.subtract, A.mult),
                    waits=[("ACT", rstd_m)])
                twaits = [("DVE", xnmil[i]), ("G", consts_done),
                          ("DVE", xntmil[i - 2] if i >= 2 else 0)]
                if b > 0 and i < 2:
                    twaits.append(("PE", prev["c_pe"]))
                for ss in range(8):
                    trmil[i] = em("tensor", lambda e, ii=i, ss=ss: e.transpose(
                        PSB[:, ((ii % 2) * 8 + ss) * 128:((ii % 2) * 8 + ss + 1) * 128],
                        BIG[:, ii * D + ss * 128: ii * D + (ss + 1) * 128], IDN[:]),
                        waits=twaits if ss == 0 else (), inc=(ss == 7))
                xntmil[i] = em("vector", lambda e, ii=i: e.tensor_copy(
                    XNT[:].rearrange("p (s t) -> p s t", s=8)[:, :, ii * 128:(ii + 1) * 128],
                    PSB[:, (ii % 2) * 1024:((ii % 2) + 1) * 1024].rearrange("p (s t) -> p s t", s=8)),
                    waits=[("PE", trmil[i])])
                for ss in range(8):
                    lgmil[i] = em("tensor", lambda e, ii=i, ss=ss: e.matmul(
                        PSF[:, 1024 + ii * 8:1024 + (ii + 1) * 8],
                        XNT[:, ss * T + ii * 128: ss * T + (ii + 1) * 128],
                        WG[:, ss * 8:(ss + 1) * 8], start=(ss == 0), stop=(ss == 7)),
                        waits=[("DVE", xntmil[i])] if ss == 0 else (), inc=(ss == 7))
                for hh in range(2):
                    m_acc_pe = em("tensor", lambda e, ii=i, hh=hh: e.matmul(
                        PSF[0:1, 1536 + hh * 512:1536 + (hh + 1) * 512], CON1[:],
                        BIG[:, ii * D + hh * 512: ii * D + hh * 512 + 512],
                        start=(ii == 0), stop=(ii == TIL - 1)),
                        inc=(hh == 1 and i == TIL - 1))
            lg_evac = em("vector", lambda e: e.tensor_copy(LGT[:], PSF[:, 1024:1280]),
                         waits=[("PE", lgmil[TIL - 1])])
            mr1_m = em("vector", lambda e: e.tensor_scalar(MR1[:], PSF[0:1, 1536:2560], 1.0 / T, None, A.mult, A.bypass),
                       waits=[("PE", m_acc_pe)])
            for hh in range(2):
                mbp = em("tensor", lambda e, hh=hh: e.matmul(
                    PSF[:, hh * 512:(hh + 1) * 512], CROW[:], MR1[0:1, hh * 512:(hh + 1) * 512],
                    start=True, stop=True),
                    waits=[("DVE", mr1_m)] if hh == 0 else (), inc=(hh == 1))
            mb_m = em("vector", lambda e: e.tensor_copy(GEN[:, MB_C:MB_C + 1024], PSF[:, 0:1024]),
                      waits=[("PE", mbp)])

            if dbg and b == 0:
                em("sync", lambda e: e.dma_start(out=dbg_t["dbg_xn"], in_=BIG[:]), waits=[("DVE", xnmil[TIL - 1])], inc=True)
                em("sync", lambda e: e.dma_start(out=dbg_t["dbg_xnt"], in_=XNT[:]), waits=[("DVE", xntmil[TIL - 1])], inc=True)
                em("sync", lambda e: e.dma_start(out=dbg_t["dbg_lgt"], in_=LGT[:]), waits=[("DVE", lg_evac)], inc=True)

            ctr_cp = [0] * TIL
            ctr_pe = [0] * TIL
            w16_last = 0
            for i in range(TIL):
                em("vector", lambda e, ii=i: e.tensor_tensor(
                    U2[:, 0:1024], BIG[:, ii * D:(ii + 1) * D], GEN[:, MB_C:MB_C + 1024], A.mult),
                    waits=[("DVE", mb_m)])
                u_m = em("vector", lambda e: e.reduce_sum(
                    UWT[:, 0:16], U2[:, 0:1024].rearrange("p (w c) -> p w c", c=64), axis=AX.X))
                em("vector", lambda e, ii=i: e.tensor_tensor(
                    U2[:, 0:1024], BIG[:, ii * D:(ii + 1) * D], BIG[:, ii * D:(ii + 1) * D], A.mult))
                n2_m = em("vector", lambda e: e.reduce_sum(
                    UWT[:, 16:32], U2[:, 0:1024].rearrange("p (w c) -> p w c", c=64), axis=AX.X))
                w_m = em("scalar", lambda e: e.activation(UWT[:, 32:48], UWT[:, 0:16], AF.Exp, scale=-1.0),
                         waits=[("DVE", u_m)])
                w16_last = em("vector", lambda e, ii=i: e.tensor_copy(W16[:, ii * 16:(ii + 1) * 16], UWT[:, 32:48]),
                              waits=[("ACT", w_m)])
                lg = LGT[:, i * 8:(i + 1) * 8]
                em("vector", lambda e, lg=lg: e.reduce_max(UWT[:, 48:49], lg, axis=AX.X))
                em("vector", lambda e, lg=lg: e.tensor_scalar(U2[:, 1024:1032], lg, UWT[:, 48:49], None, A.is_ge, A.bypass))
                em("vector", lambda e, lg=lg: e.scalar_tensor_tensor(U2[:, 1032:1040], U2[:, 1024:1032], -1e30, lg, A.mult, A.add))
                em("vector", lambda e: e.reduce_max(UWT[:, 49:50], U2[:, 1032:1040], axis=AX.X))
                em("vector", lambda e: e.tensor_scalar(U2[:, 1040:1048], U2[:, 1032:1040], UWT[:, 49:50], None, A.is_ge, A.bypass))
                d_m = em("vector", lambda e: e.tensor_tensor(UWT[:, 50:51], UWT[:, 49:50], UWT[:, 48:49], A.subtract))
                eact = em("scalar", lambda e: e.activation(UWT[:, 51:52], UWT[:, 50:51], AF.Exp),
                          waits=[("DVE", d_m)])
                em("vector", lambda e: e.tensor_scalar(UWT[:, 52:53], UWT[:, 51:52], 1.0, None, A.add, A.bypass),
                   waits=[("ACT", eact)])
                em("vector", lambda e: e.reciprocal(UWT[:, 53:54], UWT[:, 52:53]))
                em("vector", lambda e: e.tensor_tensor(UWT[:, 54:55], UWT[:, 51:52], UWT[:, 53:54], A.mult))
                em("vector", lambda e: e.tensor_scalar(U2[:, 1048:1056], U2[:, 1040:1048], UWT[:, 54:55], None, A.mult, A.bypass))
                em("vector", lambda e: e.scalar_tensor_tensor(CMB[:, 0:8], U2[:, 1024:1032], UWT[:, 53:54], U2[:, 1048:1056], A.mult, A.add),
                   waits=[("PE", ctr_pe[i - 1] if i >= 1 else 0)])
                n2c_m = em("vector", lambda e: e.tensor_copy(CMB[:, 32:48], UWT[:, 16:32]))
                ctr_pe[i] = em("tensor", lambda e, ii=i: e.transpose(
                    PSB[0:48, (ii % 2) * 128:(ii % 2) * 128 + 128],
                    CMB[:], IDN[:]),
                    waits=[("DVE", n2c_m), ("DVE", ctr_cp[i - 2] if i >= 2 else 0)], inc=True)
                ctr_cp[i] = em("vector", lambda e, ii=i: e.tensor_copy(
                    GEN[0:48, TR_C + ii * 128:TR_C + (ii + 1) * 128],
                    PSB[0:48, (ii % 2) * 128:(ii % 2) * 128 + 128]),
                    waits=[("PE", ctr_pe[i])])
            em("vector", lambda e: e.reduce_max(GAM[32:48, 4:5], GEN[32:48, TR_C:TR_C + T], axis=AX.X),
               waits=[("DVE", ctr_cp[TIL - 1])])
            gam_m = em("vector", lambda e: e.tensor_scalar(GAM[32:48, 16:17], GAM[32:48, 4:5], -70.0, None, A.add, A.bypass))
            gb_x = em("sync", lambda e: e.dma_start(out=gscr, in_=GAM[32:48, 16:17]),
                      waits=[("DVE", gam_m)], inc=True)
            gb_g = em("gpsimd", lambda e: e.dma_start(out=GRW[:], in_=gscr.rearrange("a b -> (a b)").unsqueeze(0)),
                      waits=[("X", gb_x)], inc=True)
            gbp = em("tensor", lambda e: e.matmul(PSF[:, 0:16], CNR[:], GRW[:], start=True, stop=True),
                     waits=[("G", gb_g), ("DVE", mb_m)], inc=True)
            gb_m = em("vector", lambda e: e.tensor_copy(GB[:], PSF[:, 0:16]),
                      waits=[("PE", gbp)])

            if dbg and b == 0:
                em("sync", lambda e: e.dma_start(out=dbg_t["dbg_tr"], in_=GEN[:, TR_C:TR_C + T]),
                   waits=[("DVE", ctr_cp[TIL - 1])], inc=True)
                em("sync", lambda e: e.dma_start(out=dbg_t["dbg_gb"], in_=GB[:]), waits=[("DVE", gb_m)], inc=True)
                em("sync", lambda e: e.dma_start(out=dbg_t["dbg_w16"], in_=W16[:]), waits=[("DVE", w16_last)], inc=True)

            if "B" not in phases:
                prev = {"out_x": cnt["X"], "d_dve": cnt["DVE"], "c_pe": cnt["PE"], "c_dve": cnt["DVE"]}
                continue

            prev_head_o_last = 0
            prev_head_oevac = mb_m
            attn_store = [0] * H
            for h in range(H):
                s0 = h // 2
                rbase = (h % 2) * 64
                vw_w = [("DVE", w16_last), ("PE", prev_head_o_last), ("X", attn_store[h - 1] if h >= 1 else 0)]
                for j in range(TIL):
                    em("vector", lambda e, j=j, h=h: e.tensor_scalar(
                        GEN[:, VW_C + j * 65:VW_C + j * 65 + 64],
                        BIG[:, j * D + h * HD: j * D + h * HD + 64],
                        W16[:, j * 16 + h: j * 16 + h + 1], None, A.mult, A.bypass),
                        waits=vw_w if j == 0 else ())
                vw_m = em("vector", lambda e, h=h: e.tensor_copy(
                    GEN[:, VW_C:VW_C + 65 * 32].rearrange("p (j c) -> p j c", c=65)[:, :, 64:65],
                    W16[:, 0:512].rearrange("p (j c) -> p j c", c=16)[:, :, h:h + 1]))
                smil = [0] * (NCH * TIL)
                omil = [0] * (NCH * TIL)
                expmil = [0] * (NCH * TIL)
                oevac = [0] * NCH
                def emit_s(jj):
                    ch_, j_ = divmod(jj, TIL)
                    sw = [("ACT", expmil[jj - 2] if jj >= 2 else 0)]
                    if jj == 0:
                        sw += [("DVE", vw_m), ("DVE", gb_m)]
                    for hh in range(2):
                        smil[jj] = em("tensor", lambda e, s0=s0, rb=rbase, j=j_, ch=ch_, jj=jj, hh=hh: e.matmul(
                            PSF[:, (jj % 2) * 1024 + hh * 512:(jj % 2) * 1024 + hh * 512 + 512],
                            XNT[rb:rb + 64, s0 * T + j * 128: s0 * T + (j + 1) * 128],
                            XNT[rb:rb + 64, s0 * T + ch * CH + hh * 512: s0 * T + ch * CH + hh * 512 + 512],
                            start=True, stop=True),
                            waits=sw if hh == 0 else (), inc=(hh == 1))
                emit_s(0)
                for jj in range(NCH * TIL):
                    ch, j = divmod(jj, TIL)
                    if jj + 1 < NCH * TIL:
                        emit_s(jj + 1)
                    ew = [("PE", smil[jj]),
                          ("PE", omil[jj - 2] if jj >= 2 else prev_head_o_last)]
                    expmil[jj] = em("scalar", lambda e, jj=jj, h=h: e.activation(
                        GEN[:, RS_C + (jj % 2) * 1024:RS_C + (jj % 2) * 1024 + 1024],
                        PSF[:, (jj % 2) * 1024:(jj % 2) * 1024 + 1024],
                        AF.Exp, bias=GB[:, h: h + 1]),
                        waits=ew)
                    ow = [("ACT", expmil[jj])]
                    if j == 0:
                        ow.append(("DVE", oevac[ch - 1] if ch >= 1 else prev_head_oevac))
                    for hh in range(2):
                        omil[jj] = em("tensor", lambda e, j=j, jj=jj, hh=hh: e.matmul(
                            PSF[0:65, 2048 + hh * 512:2048 + hh * 512 + 512],
                            GEN[:, VW_C + j * 65:VW_C + (j + 1) * 65],
                            GEN[:, RS_C + (jj % 2) * 1024 + hh * 512:RS_C + (jj % 2) * 1024 + hh * 512 + 512],
                            start=(j == 0), stop=(j == TIL - 1)),
                            waits=ow if hh == 0 else (), inc=(hh == 1))
                    if j == TIL - 1:
                        ovw = [("PE", omil[jj])]
                        if ch == 0:
                            ovw.append(("X", attn_store[h - 1] if h >= 1 else 0))
                        oevac[ch] = em("vector", lambda e, ch=ch: e.tensor_copy(
                            GEN[0:65, OST_C + ch * CH:OST_C + (ch + 1) * CH], PSF[0:65, 2048:3072]),
                            waits=ovw)
                prev_head_o_last = omil[NCH * TIL - 1]
                prev_head_oevac = oevac[NCH - 1]
                attn_store[h] = em("sync", lambda e, h=h: e.dma_start(
                    out=attnT[:, h, :], in_=GEN[0:65, OST_C:OST_C + T]),
                    waits=[("DVE", oevac[NCH - 1])], inc=True)
            b_last_pe = prev_head_o_last
            b_store_x = attn_store[H - 1]

            if "C" not in phases:
                prev = {"out_x": cnt["X"], "d_dve": cnt["DVE"], "c_pe": cnt["PE"], "c_dve": cnt["DVE"]}
                continue

            bp01 = ("DVE", 0)
            bp23 = ("ACT", 0)
            bp45 = ("DVE", prev_head_oevac)
            wi_slot_free = [b_last_pe, b_last_pe]
            wo_slot_free = [b_last_pe, b_last_pe]
            gbc_slot_free = [b_last_pe, b_last_pe]
            gbc_slot_free_d = [0, 0]
            yst = 0
            p2_last_pe = b_last_pe
            p2_last_dve = 0
            for ch in range(NCH):
                for ee in range(E):
                    gw = [(bp45[0], bp45[1]), ("DVE", ctr_cp[TIL - 1]), ("X", b_store_x)]
                    for hh in range(2):
                        gbp = em("tensor", lambda e, ee=ee, ch=ch, hh=hh: e.matmul(
                            PSF[:, 2048 + hh * 512:2048 + hh * 512 + 512],
                            OH[:, ee * 128:(ee + 1) * 128],
                            GEN[0:8, TR_C + ch * CH + hh * 512:TR_C + ch * CH + hh * 512 + 512],
                            start=True, stop=True),
                            waits=gw if hh == 0 else (), inc=(hh == 1))
                    gbc_m = em("vector", lambda e, ee=ee: e.tensor_copy(
                        GEN[:, GBC_C + (ee % 2) * 1024:GBC_C + (ee % 2) * 1024 + 1024], PSF[:, 2048:3072]),
                        waits=[("PE", gbp), ("DVE", gbc_slot_free_d[ee % 2])])
                    bp45 = ("DVE", gbc_m)
                    p1evac_v = 0
                    p1evac_a = 0
                    p1mil = [0] * 32
                    for ht in range(32):
                        wdm = em("gpsimd", lambda e, ee=ee, ht=ht: e.dma_start(
                            out=GEN[:, WI_C + (ht % 2) * 1024:WI_C + (ht % 2) * 1024 + 1024], in_=wih[ee, ht]),
                            waits=[("PE", wi_slot_free[ht % 2]), ("X", b_store_x)], inc=True, key="WI")
                        bp = bp01 if ht % 2 == 0 else bp23
                        mw = [("WI", wdm), (bp[0], bp[1])]
                        for k in range(8):
                            for hh in range(2):
                                p1mil[ht] = em("tensor", lambda e, ht=ht, k=k, hh=hh, ch=ch: e.matmul(
                                    PSF[:, (ht % 2) * 1024 + hh * 512:(ht % 2) * 1024 + hh * 512 + 512],
                                    GEN[:, WI_C + (ht % 2) * 1024 + k * 128:WI_C + (ht % 2) * 1024 + (k + 1) * 128],
                                    XNT[:, k * T + ch * CH + hh * 512: k * T + ch * CH + hh * 512 + 512],
                                    start=(k == 0), stop=(k == 7)),
                                    waits=mw if (k == 0 and hh == 0) else (), inc=(k == 7 and hh == 1))
                        wi_slot_free[ht % 2] = p1mil[ht]
                        evw = [("PE", p1mil[ht])]
                        if ht < 2:
                            evw.append(("PE", p2_last_pe))
                        if ht % 2 == 0:
                            p1evac_v = em("vector", lambda e, ht=ht: e.tensor_copy(
                                BIG[:, ht * 1024:(ht + 1) * 1024], PSF[:, 0:1024]), waits=evw)
                            bp01 = ("DVE", p1evac_v)
                        else:
                            p1evac_a = em("scalar", lambda e, ht=ht: e.copy(
                                BIG[:, ht * 1024:(ht + 1) * 1024], PSF[:, 1024:2048]), waits=evw)
                            bp23 = ("ACT", p1evac_a)
                    for dt in range(8):
                        wdm = em("gpsimd", lambda e, ee=ee, dt=dt: e.dma_start(
                            out=GEN[:, WO_C + (dt % 2) * 4096:WO_C + (dt % 2) * 4096 + 4096], in_=woh[ee, dt]),
                            waits=[("PE", wo_slot_free[dt % 2]), ("X", b_store_x)], inc=True)
                        bp = bp45 if dt % 2 == 0 else bp23
                        mw = [("G", wdm), (bp[0], bp[1]),
                              ("DVE", p1evac_v), ("ACT", p1evac_a)]
                        pcol = 2048 if dt % 2 == 0 else 1024
                        p2m = 0
                        for sth in range(32):
                            for hh in range(2):
                                p2m = em("tensor", lambda e, dt=dt, sth=sth, hh=hh, pcol=pcol: e.matmul(
                                    PSF[:, pcol + hh * 512:pcol + hh * 512 + 512],
                                    GEN[:, WO_C + (dt % 2) * 4096 + sth * 128:WO_C + (dt % 2) * 4096 + (sth + 1) * 128],
                                    BIG[:, sth * 1024 + hh * 512: sth * 1024 + hh * 512 + 512],
                                    start=(sth == 0), stop=(sth == 31)),
                                    waits=mw if (sth == 0 and hh == 0) else (), inc=(sth == 31 and hh == 1))
                        wo_slot_free[dt % 2] = p2m
                        p2_last_pe = p2m
                        ew = [("PE", p2m), ("DVE", gbc_m)]
                        if ee == 0:
                            ew.append(("X", yst))
                        em("vector", lambda e, pcol=pcol, ee=ee: e.tensor_tensor(
                            U2[:, 0:1024], PSF[:, pcol:pcol + 1024],
                            GEN[:, GBC_C + (ee % 2) * 1024:GBC_C + (ee % 2) * 1024 + 1024], A.mult),
                            waits=ew)
                        if ee == 0:
                            p2ev = em("vector", lambda e, dt=dt: e.tensor_copy(
                                YAC[:, dt * 1024:(dt + 1) * 1024], U2[:, 0:1024]))
                        else:
                            p2ev = em("vector", lambda e, dt=dt: e.tensor_tensor(
                                YAC[:, dt * 1024:(dt + 1) * 1024], YAC[:, dt * 1024:(dt + 1) * 1024],
                                U2[:, 0:1024], A.add))
                        if dt % 2 == 0:
                            bp45 = ("DVE", p2ev)
                        else:
                            bp23 = ("DVE", p2ev)
                        p2_last_dve = p2ev
                    gbc_slot_free_d[ee % 2] = p2_last_dve
                yst = em("sync", lambda e, ch=ch: e.dma_start(
                    out=yT[:, ch * CH:(ch + 1) * CH].rearrange("(dt p) t -> p dt t", p=128),
                    in_=YAC[:].rearrange("p (dt t) -> p dt t", t=CH)),
                    waits=[("DVE", p2_last_dve)], inc=True)
            c_last_pe = p2_last_pe
            c_last_dve = p2_last_dve
            c_last_x = yst

            if "D" not in phases:
                prev = {"out_x": cnt["X"], "d_dve": cnt["DVE"], "c_pe": cnt["PE"], "c_dve": cnt["DVE"]}
                continue

            attnT_r = attnT
            ain_mil = [0, 0]
            yin_mil = [0, 0]
            ain_free = [c_last_pe, c_last_pe]
            yin_free = [c_last_pe, c_last_pe]
            acc_free = [0, 0]
            xt_free = [c_last_dve, c_last_dve]
            d_dve = c_last_dve
            out_x = c_last_x
            atr_last = [0, 0]
            ytr_last = [0, 0]
            yadd_last = [0, 0]
            AIN_CS = (AIN_C0, AIN_C1)
            YIN_CS = (YIN_C0, YIN_C1)
            for pp in range(16):
                ps = pp % 2
                ain_mil[ps] = em("sync", lambda e, pp=pp, ps=ps: e.dma_start(
                    out=GEN[0:65, AIN_CS[ps]:AIN_CS[ps] + 4096].rearrange("r (h t) -> r h t", t=256),
                    in_=attnT_r[:, :, pp * 256:(pp + 1) * 256]),
                    waits=[("PE", ain_free[ps]), ("X", c_last_x), ("DVE", c_last_dve)], inc=True)
                yin_mil[ps] = em("sync", lambda e, pp=pp, ps=ps: e.dma_start(
                    out=GEN[:, YIN_CS[ps]:YIN_CS[ps] + 2048].rearrange("p (s t) -> p s t", t=256),
                    in_=yT[:, pp * 256:(pp + 1) * 256].rearrange("(s p) t -> p s t", p=128)),
                    waits=[("PE", yin_free[ps]), ("X", c_last_x), ("DVE", c_last_dve)], inc=True)
                for t2 in range(2):
                    i = pp * 2 + t2
                    isl = i % 2
                    xm = em("sync", lambda e, bb=b, ii=i, isl=isl: e.dma_start(
                        out=XT[:, isl * 1024:(isl + 1) * 1024], in_=x[bb, ii * 128:(ii + 1) * 128, :]),
                        waits=[("DVE", xt_free[isl])], inc=True)
                    tw = [("X", xm), ("DVE", yadd_last[1 - isl] if i >= 1 else 0),
                          ("DVE", d_dve if i == 0 else 0)]
                    atr = 0
                    for h in range(H):
                        atr = em("tensor", lambda e, h=h, ps=ps, t2=t2: e.transpose(
                            PSB[:, h * 128:h * 128 + 65],
                            GEN[0:65, AIN_CS[ps] + h * 256 + t2 * 128:AIN_CS[ps] + h * 256 + t2 * 128 + 128],
                            IDN[0:65, 0:65]),
                            waits=tw if h == 0 else (), inc=(h == H - 1))
                    atr_last[isl] = atr
                    ain_free[ps] = atr
                    for h in range(H):
                        em("vector", lambda e, h=h: e.reciprocal(RCP[:, h:h + 1], PSB[:, h * 128 + 64:h * 128 + 65]),
                           waits=[("PE", atr)] if h == 0 else ())
                    st_m = 0
                    for h in range(H):
                        st_m = em("vector", lambda e, h=h, isl=isl: e.scalar_tensor_tensor(
                            U2[:, isl * 1024 + h * 64: isl * 1024 + (h + 1) * 64],
                            PSB[:, h * 128:h * 128 + 64], RCP[:, h:h + 1],
                            XT[:, isl * 1024 + h * 64: isl * 1024 + (h + 1) * 64], A.mult, A.add),
                            waits=[("X", xm), ("SO", acc_free[isl])] if h == 0 else ())
                    xt_free[isl] = st_m
                    ytr = 0
                    for sth in range(8):
                        ytr = em("tensor", lambda e, sth=sth, ps=ps, t2=t2: e.transpose(
                            PSB[:, sth * 128:(sth + 1) * 128],
                            GEN[:, YIN_CS[ps] + sth * 256 + t2 * 128:YIN_CS[ps] + sth * 256 + t2 * 128 + 128],
                            IDN[:]),
                            waits=[("DVE", st_m), ("X", xm)] if sth == 0 else (), inc=(sth == 7))
                    ytr_last[isl] = ytr
                    yin_free[ps] = ytr
                    ya = 0
                    for sth in range(8):
                        ya = em("vector", lambda e, sth=sth, isl=isl: e.tensor_tensor(
                            U2[:, isl * 1024 + sth * 128: isl * 1024 + (sth + 1) * 128],
                            U2[:, isl * 1024 + sth * 128: isl * 1024 + (sth + 1) * 128],
                            PSB[:, sth * 128:(sth + 1) * 128], A.add),
                            waits=[("PE", ytr)] if sth == 0 else ())
                    yadd_last[isl] = ya
                    d_dve = ya
                    acc_free[isl] = em("sync", lambda e, bb=b, ii=i, isl=isl: e.dma_start(
                        out=out[bb, ii * 128:(ii + 1) * 128, :], in_=U2[:, isl * 1024:(isl + 1) * 1024]),
                        waits=[("DVE", ya)], inc=True, key="SO")
                    out_x = acc_free[isl]
            prev = {"out_x": out_x, "d_dve": d_dve, "c_pe": cnt["PE"], "c_dve": cnt["DVE"]}

        em("sync", lambda e: e.nop(), waits=[("X", cnt["X"]), ("SO", cnt["SO"])])
        em("gpsimd", lambda e: e.nop(), waits=[("G", cnt["G"]), ("WI", cnt["WI"])])

        @block.sync
        def _(eng):
            for f in plan["sync"]:
                f(eng, sems)

        @block.gpsimd
        def _(eng):
            for f in plan["gpsimd"]:
                f(eng, sems)

        @block.tensor
        def _(eng):
            for f in plan["tensor"]:
                f(eng, sems)

        @block.scalar
        def _(eng):
            for f in plan["scalar"]:
                f(eng, sems)

        @block.vector
        def _(eng):
            for f in plan["vector"]:
                f(eng, sems)

    return nc


def host_inputs(L0_attn_x, L0_ffn_w_gate, L0_ffn_w_in, L0_ffn_w_out):
    xf = np.ascontiguousarray(np.asarray(L0_attn_x, dtype=np.float32))
    wg = np.asarray(L0_ffn_w_gate, dtype=np.float32)
    wi = np.asarray(L0_ffn_w_in, dtype=np.float32)
    wo = np.asarray(L0_ffn_w_out, dtype=np.float32)
    WIH = np.ascontiguousarray(wi.reshape(E, 32, 128, 8, 128).transpose(0, 1, 4, 3, 2)).reshape(E, 32, 128, 1024)
    WOH = np.ascontiguousarray(wo.reshape(E, 8, 128, 32, 128).transpose(0, 1, 4, 3, 2)).reshape(E, 8, 128, 4096)
    WGH = np.ascontiguousarray(wg.reshape(8, 8, 128).transpose(2, 1, 0)).reshape(128, 64)
    ident = np.eye(128, dtype=np.float32)
    oh = np.zeros((8, 1024), dtype=np.float32)
    for e in range(8):
        oh[e, e * 128:(e + 1) * 128] = 1.0
    cones = np.ones((128, 1), dtype=np.float32)
    crow = np.ones((1, 128), dtype=np.float32)
    cnrow = -np.ones((1, 128), dtype=np.float32)
    shards = [np.ascontiguousarray(xf[2 * c:2 * c + 2]) for c in range(8)]
    in_maps = [{"x": shards[c], "wih": WIH, "woh": WOH, "wgh": WGH, "ident": ident,
                "oh": oh, "cones": cones, "crow": crow, "cnrow": cnrow} for c in range(8)]
    return in_maps


def kernel(L0_attn_x, L0_ffn_w_gate, L0_ffn_w_in, L0_ffn_w_out):
    in_maps = host_inputs(L0_attn_x, L0_ffn_w_gate, L0_ffn_w_in, L0_ffn_w_out)
    nc = build_program()
    res = run_bass_kernel_spmd(nc, in_maps, core_ids=list(range(8)))
    return np.concatenate([np.asarray(r["out"]) for r in res.results], axis=0)
